# version 02 (final)
# baseline (speedup 1.0000x reference)
import os
import contextlib
import numpy as np
import ml_dtypes
import concourse.bass as bass
import concourse.mybir as mybir
from concourse.bass_utils import run_bass_kernel_spmd

F32 = mybir.dt.float32
BF16 = mybir.dt.bfloat16
AF = mybir.ActivationFunctionType
ALU = mybir.AluOpType
AX = mybir.AxisListType
bfdt = ml_dtypes.bfloat16

NCORE = 8
T = 1024
TH = T + 3
NCH = 8
CS = 128
DM = 4096
DIN = 8192
NH = 128
HD = 64
G = 8
CONV = 10240
PROJ_XBCDT = CONV + NH
KC = DM // 128
KD = DIN // 128
EPS = 1e-5
MASKBIG = -32768.0

_UPTO = int(os.environ.get("MK_UPTO", "99"))
_DEBUG = int(os.environ.get("MK_DEBUG", "0"))


class Ctr:
    def __init__(self, h):
        self.h = h
        self.n = 0

    def bump(self, ins, k):
        ins.then_inc(self.h, k)
        self.n += k


def build_program(part=1, debug=False):
    nc = bass.Bass("TRN2", target_bir_lowering=False)
    es = contextlib.ExitStack()

    def din(name, shape, dt):
        return nc.dram_tensor(name, shape, dt, kind="ExternalInput").ap()

    def dscr(name, shape, dt, xfer=None):
        if xfer == "out":
            kind = "ExternalOutput" if part == 1 else "ExternalInput"
        else:
            kind = "ExternalOutput" if debug else "Internal"
        return nc.dram_tensor(name, shape, dt, kind=kind).ap()

    x_in = din("x_in", [TH, DM], F32)
    wt_xbcdt = din("wt_xbcdt", [DM, PROJ_XBCDT], F32)
    wt_gate = din("wt_gate", [DM, DIN], F32)
    wt_out = din("wt_out", [DIN, DM], F32)
    wt_base = din("wt_base", [DM, DM], F32)
    wt_gatew = din("wt_gatew", [DM, 8], F32)
    wt_lA = din("wt_lA", [DM, 128], F32)
    wt_lB = din("wt_lB", [128, DM], F32)
    cw_in = din("cw_in", [128, 320], F32)
    cb_in = din("cb_in", [128, 80], F32)
    dtb_in = din("dtb_in", [128, 1], F32)
    alog_in = din("alog_in", [128, 1], F32)
    ddiag_in = din("ddiag_in", [DIN, 128], F32)
    nw_in = din("nw_in", [1, DIN], F32)
    bb_in = din("bb_in", [1, DM], F32)
    ident_in = din("ident_in", [128, 128], BF16)
    identf_in = din("identf_in", [128, 128], F32)
    drow_in = din("drow_in", [1, 128], F32)
    maskl_in = din("maskl_in", [128, 128], BF16)
    maskr_in = din("maskr_in", [128, 128], BF16)
    emat_in = din("emat_in", [8, 128], BF16)
    m8_in = din("m8_in", [1, 8], F32)
    m8c_in = din("m8c_in", [1, 8], F32)

    out = nc.dram_tensor("out", [T, DM], F32, kind="ExternalOutput").ap()

    xp_cT = dscr("xp_cT", [128, G * NCH * 128], BF16, xfer="out")
    xp_sdecT = dscr("xp_sdecT", [128, T], F32, xfer="out")
    xp_cdrow = dscr("xp_cdrow", [1, NCH * 128], BF16, xfer="out")
    scr_xsilu = dscr("scr_xsilu", [CONV, T], BF16)
    scr_xs = dscr("scr_xs", [T, DIN], BF16)
    scr_Ypart = dscr("scr_Ypart", [T, DIN], BF16, xfer="out")
    scr_S = dscr("scr_S", [T, DIN], BF16, xfer="out")
    scr_gate = dscr("scr_gate", [T, DIN], BF16, xfer="out")
    scr_yT = dscr("scr_yT", [DIN, T], BF16)
    scr_attn = dscr("scr_attn", [T, DM], F32)
    scr_x1 = dscr("scr_x1", [T, DM], F32)
    scr_flat = nc.dram_tensor("scr_flat", [4, NH * CS], BF16, kind="Internal").ap()
    scr_cd = nc.dram_tensor("scr_cd", [NCH, NH], BF16, kind="Internal").ap()
    if part == 1:
        cc_in = nc.dram_tensor("cc_in", [129, DIN], BF16, kind="ExternalOutput").ap()
    else:
        cc_out = nc.dram_tensor("cc_out", [129 * NCORE, DIN], BF16, kind="ExternalInput").ap()
    dbg_dt = dscr("dbg_dt", [128, T], F32)
    dbg_cum = dscr("dbg_cum", [128, T], F32)
    dbg_R = dscr("dbg_R", [128, DIN], BF16)
    dbg_L = dscr("dbg_L", [128, 512], BF16)
    dbg_Lw = dscr("dbg_Lw", [4, 8192], BF16)
    dbg_Gm = dscr("dbg_Gm", [128, 1024], BF16)
    dbg_ln = dscr("dbg_ln", [128, 2 * TH], BF16)

    sb = lambda name, shape, dt: es.enter_context(nc.sbuf_tensor(name, shape, dt))
    mksem = lambda name: Ctr(es.enter_context(nc.semaphore(name)))

    ln1T = sb("ln1T", [128, KC * TH], BF16)
    sdecT = sb("sdecT", [128, T], F32)
    cdrow_bf = sb("cdrow_bf", [1, NCH * 128], BF16)
    cT_all = sb("cT_all", [128, G * NCH * 128], BF16)
    Rst = sb("Rst", [128, DIN], BF16)
    identb = sb("identb", [128, 128], BF16)
    emat = sb("emat", [8, 128], BF16)
    ones1 = sb("ones1", [1, 128], BF16)
    zcol = sb("zcol", [128, 1], F32)
    epscol = sb("epscol", [128, 1], F32)
    m8 = sb("m8", [1, 8], F32)
    m8c = sb("m8c", [1, 8], F32)
    if part == 1:
        dt_fm = sb("dt_fm", [128, T], F32)
        cumAH = sb("cumAH", [128, T], F32)
        identf = sb("identf", [128, 128], F32)
        maskl = sb("maskl", [128, 128], BF16)
        maskr = sb("maskr", [128, 128], BF16)
        cw = sb("cw", [128, 320], F32)
        cb = sb("cb", [128, 80], F32)
        dtb = sb("dtb", [128, 1], F32)
        negA = sb("negA", [128, 1], F32)
        zbig = sb("zbig", [128, T], F32)

    st_s = sb("st_s", [128, 1], F32)
    st_ss = sb("st_ss", [128, 1], F32)
    st_m = sb("st_m", [128, 1], F32)
    st_t = sb("st_t", [128, 1], F32)
    st_v = sb("st_v", [128, 1], F32)
    st_r = sb("st_r", [128, 1], F32)

    s_cast = mksem("s_cast")
    s_load = mksem("s_load")
    s_spill = mksem("s_spill")
    s_pe = mksem("s_pe")
    s_ev = mksem("s_ev")
    s_ac = mksem("s_ac")
    s_ve = mksem("s_ve")
    s_tp = mksem("s_tp")
    s_aux = mksem("s_aux")
    s_cc = mksem("s_cc")
    s_out = mksem("s_out")


    def emit_part2():
        nw_wide = sb("nw_wide", [128, DIN], BF16)
        scr_yT2 = nc.dram_tensor("scr_yT2", [DIN, T], BF16, kind="Internal").ap()
        scr_attn2 = dscr("scr_attn2", [T, DM], F32)
        scr_x12 = dscr("scr_x12", [T, DM], F32)
        scr_ffn = dscr("scr_ffn", [T, DM], BF16)

        with contextlib.ExitStack() as pes:
            pp = lambda name, shape, dt: pes.enter_context(nc.psum_tensor(name, shape, dt))
            ps_nw = [pp(f"ps_nw{i}", [128, 512], F32) for i in range(4)]
            nwrow = pes.enter_context(nc.sbuf_tensor("nwrow", [1, DIN], BF16))
            with nc.Block("init2") as blk:
                b_load, b_cast, b_pe, b_ev = s_load.n, s_cast.n, s_pe.n, s_ev.n

                @blk.sync
                def _(sy):
                    for dst, srcap in [(identb, ident_in), (emat, emat_in),
                                       (m8, m8_in), (m8c, m8c_in),
                                       (cT_all, xp_cT), (sdecT, xp_sdecT),
                                       (cdrow_bf, xp_cdrow)]:
                        s_load.bump(sy.dma_start(dst[:], srcap), 16)

                @blk.gpsimd
                def _(g):
                    s_cast.bump(g.dma_start(nwrow[:], nw_in), 16)

                @blk.vector
                def _(v):
                    v.memset(ones1[:], 1.0)
                    v.memset(zcol[:], 0.0)
                    v.memset(epscol[:], EPS)
                    v.memset(Rst[:], 0.0)

                @blk.tensor
                def _(te):
                    te.wait_ge(s_cast.h, b_cast + 16)
                    for i in range(16):
                        if i >= 4:
                            te.wait_ge(s_ev.h, b_ev + i - 3)
                        s_pe.bump(te.matmul(ps_nw[i % 4][:], ones1[:],
                                            nwrow[:, i * 512:(i + 1) * 512],
                                            start=True, stop=True), 1)

                @blk.scalar
                def _(s):
                    for i in range(16):
                        s.wait_ge(s_pe.h, b_pe + i + 1)
                        s_ev.bump(s.copy(nw_wide[:, i * 512:(i + 1) * 512], ps_nw[i % 4][:]), 1)

        with contextlib.ExitStack() as pes:
            ls = lambda name, shape, dt: pes.enter_context(nc.sbuf_tensor(name, shape, dt))
            pp = lambda name, shape, dt: pes.enter_context(nc.psum_tensor(name, shape, dt))
            Rin = [ls(f"Rin{i}", [128, DIN], BF16) for i in range(2)]
            tbuf = ls("tbuf", [128, DIN], BF16)
            Drow = ls("Drow", [1, 128], BF16)
            runrow = ls("runrow", [1, 128], F32)
            wrow = ls("wrow", [1, 128], F32)
            wrowbf = ls("wrowbf", [1, 128], BF16)
            dmod = ls("dmod", [1, 128], F32)
            psC2 = pp("psC2", [128, 128], F32)
            with nc.Block("C0") as blk:
                b_load, b_ve, b_ac, b_tp = s_load.n, s_ve.n, s_ac.n, s_tp.n
                JL = list(range(NCORE - 2, -1, -1))

                @blk.sync
                def _(sy):
                    for idx, j in enumerate(JL):
                        if idx >= 2:
                            sy.wait_ge(s_ve.h, b_ve + 3 * (idx - 1))
                        s_load.bump(sy.dma_start(Rin[idx % 2][:],
                                                 cc_out[129 * j:129 * j + 128, :]), 16)
                        if idx >= 1:
                            sy.wait_ge(s_ve.h, b_ve + 3 * idx)
                        s_load.bump(sy.dma_start(Drow[:],
                                                 cc_out[129 * j + 128:129 * j + 129, 0:128]), 16)

                @blk.vector
                def _(v):
                    v.memset(runrow[:], 1.0)
                    for idx, j in enumerate(JL):
                        v.tensor_scalar(wrow[:], runrow[:], m8[0:1, j:j + 1], None, ALU.mult)
                        v.drain()
                        s_ve.bump(v.nop(), 1)
                        v.wait_ge(s_tp.h, b_tp + idx + 1)
                        v.wait_ge(s_load.h, b_load + 32 * idx + 16)
                        dupc = psC2[:].unsqueeze(2).broadcast_to([128, 128, 64])
                        v.tensor_tensor(tbuf[:].rearrange("p (h d) -> p h d", h=128),
                                        Rin[idx % 2][:].rearrange("p (h d) -> p h d", h=128),
                                        dupc, ALU.mult)
                        s_ve.bump(v.tensor_tensor(Rst[:], Rst[:], tbuf[:], ALU.add), 1)
                        v.wait_ge(s_load.h, b_load + 32 * idx + 32)
                        v.tensor_scalar(dmod[:], Drow[:], m8[0:1, j:j + 1],
                                        m8c[0:1, j:j + 1], ALU.mult, ALU.add)
                        v.drain()
                        s_ve.bump(v.tensor_tensor(runrow[:], runrow[:], dmod[:], ALU.mult), 1)
                        v.drain()

                @blk.scalar
                def _(s):
                    for idx, j in enumerate(JL):
                        s.wait_ge(s_ve.h, b_ve + 3 * idx + 1)
                        s.copy(wrowbf[:], wrow[:])
                        s.drain()
                        s_ac.bump(s.nop(), 1)

                @blk.tensor
                def _(te):
                    for idx, j in enumerate(JL):
                        te.wait_ge(s_ac.h, b_ac + idx + 1)
                        if idx >= 1:
                            te.wait_ge(s_ve.h, b_ve + 3 * idx - 1)
                        s_tp.bump(te.matmul(psC2[:], ones1[:], wrowbf[:],
                                            start=True, stop=True), 1)

        with contextlib.ExitStack() as pes:
            ls = lambda name, shape, dt: pes.enter_context(nc.sbuf_tensor(name, shape, dt))
            pp = lambda name, shape, dt: pes.enter_context(nc.psum_tensor(name, shape, dt))
            Ypin2 = ls("Ypin2", [128, DIN], BF16)
            gbuf2 = ls("gbuf2", [128, DIN], BF16)
            Sbuf2 = ls("Sbuf2", [128, DIN], BF16)
            Ywork = ls("Ywork", [128, 4096], F32)
            ytb = ls("ytb", [128, DIN], BF16)
            sqd = ls("sqd", [128, 1024], F32)
            vcols = ls("vcols", [128, 8], F32)
            rstd8 = ls("rstd8", [128, 8], F32)
            psYo = [pp(f"psYo{i}", [128, 512], F32) for i in range(2)]
            psT3 = [pp(f"psT3_{i}", [128, 128], BF16) for i in range(2)]
            psC3 = pp("psC3", [128, 128], F32)
            with nc.Block("C") as blk:
                b_load, b_sp = s_load.n, s_spill.n
                b_tp, b_ev, b_ac, b_ve = s_tp.n, s_ev.n, s_ac.n, s_ve.n
                PL2, PS2 = 48, 16
                PT2 = 16 + 1 + 64
                PE2 = 2 + 64
                PA2 = 2
                PV2 = 23
                evbump = lambda m: m + 1 if m < 8 else m + 4

                @blk.sync
                def _(sy):
                    for c in range(NCH):
                        if c >= 1:
                            sy.wait_ge(s_ve.h, b_ve + PV2 * c)
                            sy.wait_ge(s_tp.h, b_tp + PT2 * c)
                            sy.wait_ge(s_spill.h, b_sp + PS2 * c)
                        s_load.bump(sy.dma_start(Ypin2[:], scr_Ypart[c * 128:(c + 1) * 128, :]), 16)
                        s_load.bump(sy.dma_start(gbuf2[:], scr_gate[c * 128:(c + 1) * 128, :]), 16)
                        s_load.bump(sy.dma_start(Sbuf2[:], scr_S[c * 128:(c + 1) * 128, :]), 16)
                        sy.wait_ge(s_ev.h, b_ev + PE2 * (c + 1))
                        s_spill.bump(sy.dma_start(
                            scr_yT2.rearrange("(a p) t -> p a t", p=128)[:, :, c * 128:(c + 1) * 128],
                            ytb[:].rearrange("p (a t) -> p a t", a=64)), 16)

                @blk.tensor
                def _(te):
                    for c in range(NCH):
                        base_v = b_ve + PV2 * c
                        te.wait_ge(s_load.h, b_load + PL2 * c + 48)
                        for m in range(16):
                            gidx = m // 2
                            if m >= 2:
                                te.wait_ge(s_ve.h, base_v + evbump(m - 2))
                            s_tp.bump(te.matmul(
                                psYo[m % 2][:],
                                cT_all[:, (gidx * NCH + c) * 128:(gidx * NCH + c + 1) * 128],
                                Rst[:, m * 512:(m + 1) * 512],
                                start=True, stop=True), 1)
                        if c >= 1:
                            te.wait_ge(s_ve.h, b_ve + PV2 * c)
                        s_tp.bump(te.matmul(psC3[:], ones1[:],
                                            cdrow_bf[0:1, c * 128:(c + 1) * 128],
                                            start=True, stop=True), 1)
                        for i in range(64):
                            half = i // 32
                            te.wait_ge(s_ve.h, base_v + (11 if half == 0 else 22))
                            if i >= 2:
                                te.wait_ge(s_ev.h, b_ev + PE2 * c + 2 + i - 1)
                            elif c >= 1:
                                te.wait_ge(s_ev.h, b_ev + PE2 * c)
                            s_tp.bump(te.transpose(psT3[i % 2][:],
                                                   Ypin2[:, i * 128:(i + 1) * 128],
                                                   identb[:]), 1)

                @blk.vector
                def _(v):
                    for c in range(NCH):
                        base_t = b_tp + PT2 * c
                        for half in range(2):
                            for mh in range(8):
                                m = half * 8 + mh
                                v.wait_ge(s_tp.h, base_t + m + 1)
                                sdup = sdecT[:, c * 128 + m * 8:c * 128 + (m + 1) * 8] \
                                    .unsqueeze(2).broadcast_to([128, 8, 64])
                                v.tensor_tensor(
                                    Ywork[:, mh * 512:(mh + 1) * 512].rearrange(
                                        "p (a d) -> p a d", a=8),
                                    psYo[m % 2][:].rearrange("p (a d) -> p a d", a=8),
                                    sdup, ALU.mult)
                                s_ve.bump(v.tensor_tensor(
                                    Ywork[:, mh * 512:(mh + 1) * 512],
                                    Ywork[:, mh * 512:(mh + 1) * 512],
                                    Ypin2[:, m * 512:(m + 1) * 512], ALU.add), 1)
                            v.wait_ge(s_ac.h, b_ac + PA2 * c + half + 1)
                            s_ve.bump(v.tensor_tensor(
                                Ywork[:], Ywork[:],
                                gbuf2[:, half * 4096:(half + 1) * 4096], ALU.mult), 1)
                            for gg in range(4):
                                v.memset(vcols[:, half * 4 + gg:half * 4 + gg + 1], 0.0)
                                v.scalar_tensor_tensor(
                                    sqd[:], Ywork[:, gg * 1024:(gg + 1) * 1024], 1.0,
                                    Ywork[:, gg * 1024:(gg + 1) * 1024], ALU.mult, ALU.mult,
                                    accum_out=vcols[:, half * 4 + gg:half * 4 + gg + 1])
                            v.drain()
                            s_ve.bump(v.nop(), 1)
                            v.wait_ge(s_ev.h, b_ev + PE2 * c + half + 1)
                            for gg in range(4):
                                g8 = half * 4 + gg
                                v.scalar_tensor_tensor(
                                    Ypin2[:, g8 * 1024:(g8 + 1) * 1024],
                                    Ywork[:, gg * 1024:(gg + 1) * 1024],
                                    rstd8[:, g8:g8 + 1],
                                    nw_wide[:, g8 * 1024:(g8 + 1) * 1024],
                                    ALU.mult, ALU.mult)
                            v.drain()
                            s_ve.bump(v.nop(), 1)
                        v.wait_ge(s_tp.h, base_t + 17)
                        dupc = psC3[:].unsqueeze(2).broadcast_to([128, 128, 64])
                        v.tensor_tensor(Rst[:].rearrange("p (h d) -> p h d", h=128),
                                        Rst[:].rearrange("p (h d) -> p h d", h=128),
                                        dupc, ALU.mult)
                        s_ve.bump(v.tensor_tensor(Rst[:], Rst[:], Sbuf2[:], ALU.add), 1)

                @blk.scalar
                def _(s):
                    for c in range(NCH):
                        base_v = b_ve + PV2 * c
                        for half in range(2):
                            s.wait_ge(s_load.h, b_load + PL2 * c + 32)
                            if c >= 1 and half == 0:
                                s.wait_ge(s_ve.h, base_v)
                            s_ac.bump(s.activation(gbuf2[:, half * 4096:(half + 1) * 4096],
                                                   gbuf2[:, half * 4096:(half + 1) * 4096],
                                                   AF.Silu), 1)
                        for half in range(2):
                            s.wait_ge(s_ve.h, base_v + (10 if half == 0 else 21))
                            sl = slice(half * 4, half * 4 + 4)
                            s.activation(rstd8[:, sl], vcols[:, sl], AF.Ln, bias=epscol[:],
                                         scale=1.0 / 1024)
                            s.drain()
                            s.activation(rstd8[:, sl], rstd8[:, sl], AF.Exp, scale=-0.5)
                            s.drain()
                            s_ev.bump(s.nop(), 1)
                        for i in range(64):
                            s.wait_ge(s_tp.h, b_tp + PT2 * c + 17 + i + 1)
                            s_ev.bump(s.copy(ytb[:, i * 128:(i + 1) * 128], psT3[i % 2][:]), 1)


        with contextlib.ExitStack() as pes:
            ls = lambda name, shape, dt: pes.enter_context(nc.sbuf_tensor(name, shape, dt))
            pp = lambda name, shape, dt: pes.enter_context(nc.psum_tensor(name, shape, dt))
            wtile = [ls(f"wtile{i}", [128, 512], BF16) for i in range(2)]
            ytile = [ls(f"ytile{i}", [128, 1024], BF16) for i in range(2)]
            obuf = [ls(f"obuf{i}", [128, 512], F32) for i in range(2)]
            psO = [pp(f"psO{i}", [128, 512], F32) for i in range(8)]
            wo = wt_out.rearrange("(k p) c -> p k c", p=128)
            yo = scr_yT2.rearrange("(k p) t -> p k t", p=128)
            with nc.Block("C2") as blk:
                b_load, b_cast, b_sp = s_load.n, s_cast.n, s_spill.n
                b_tp, b_ev = s_tp.n, s_ev.n

                @blk.gpsimd
                def _(g):
                    for cht in range(8):
                        for k in range(KD):
                            q = cht * KD + k
                            if q >= 2:
                                g.wait_ge(s_tp.h, b_tp + q - 1)
                            s_cast.bump(g.dma_start(wtile[q % 2][:],
                                                    wo[:, k, cht * 512:(cht + 1) * 512]), 16)

                @blk.sync
                def _(sy):
                    for cht in range(8):
                        for k in range(KD):
                            q = cht * KD + k
                            if q >= 2:
                                sy.wait_ge(s_tp.h, b_tp + q - 1)
                            s_load.bump(sy.dma_start(ytile[q % 2][:], yo[:, k, :]), 16)
                        for tc in range(8):
                            e = cht * 8 + tc
                            sy.wait_ge(s_ev.h, b_ev + e + 1)
                            s_spill.bump(sy.dma_start(
                                scr_attn2[tc * 128:(tc + 1) * 128, cht * 512:(cht + 1) * 512],
                                obuf[e % 2][:]), 16)

                @blk.tensor
                def _(te):
                    for cht in range(8):
                        if cht >= 1:
                            te.wait_ge(s_ev.h, b_ev + 8 * cht)
                        for k in range(KD):
                            q = cht * KD + k
                            te.wait_ge(s_cast.h, b_cast + 16 * (q + 1))
                            te.wait_ge(s_load.h, b_load + 16 * (q + 1))
                            mm = None
                            for tc in range(8):
                                mm = te.matmul(psO[tc][:],
                                               ytile[q % 2][:, tc * 128:(tc + 1) * 128],
                                               wtile[q % 2][:],
                                               start=(k == 0), stop=(k == KD - 1))
                            s_tp.bump(mm, 1)

                @blk.scalar
                def _(s):
                    for cht in range(8):
                        for tc in range(8):
                            e = cht * 8 + tc
                            s.wait_ge(s_tp.h, b_tp + (cht + 1) * KD)
                            if e >= 2:
                                s.wait_ge(s_spill.h, b_sp + 16 * (e - 1))
                            s_ev.bump(s.copy(obuf[e % 2][:], psO[tc][:]), 1)

        AallT = sb("AallT", [128, KC * 128], BF16)
        BallT = sb("BallT", [128, DM], BF16)
        gatewT = sb("gatewT", [128, KC * 8], BF16)
        bbrow = sb("bbrow", [1, DM], BF16)
        lowT = sb("lowT", [128, T], BF16)
        lowwT = sb("lowwT", [128, T], BF16)
        with nc.Block("initD") as blk:
            b_cast = s_cast.n

            @blk.gpsimd
            def _(g):
                s_cast.bump(g.dma_start(AallT[:].rearrange("p (k e) -> p k e", k=KC),
                                        wt_lA.rearrange("(k p) e -> p k e", p=128)), 16)
                s_cast.bump(g.dma_start(BallT[:], wt_lB), 16)
                s_cast.bump(g.dma_start(gatewT[:].rearrange("p (k e) -> p k e", k=KC),
                                        wt_gatew.rearrange("(k p) e -> p k e", p=128)), 16)
                s_cast.bump(g.dma_start(bbrow[:], bb_in), 16)
                g.wait_ge(s_cast.h, s_cast.n)

        with contextlib.ExitStack() as pes:
            ls = lambda name, shape, dt: pes.enter_context(nc.sbuf_tensor(name, shape, dt))
            pp = lambda name, shape, dt: pes.enter_context(nc.psum_tensor(name, shape, dt))
            attnb0 = ls("attnb0", [128, DM], F32)
            attnb = [attnb0, attnb0]
            xorig0 = ls("xorig0", [128, DM], F32)
            xorig = [xorig0, xorig0]
            x1b0 = ls("x1b0", [128, DM], F32)
            x1b = [x1b0, x1b0]
            lnbuf2_0 = ls("lnbuf2_0", [128, DM], BF16)
            lnbuf2 = [lnbuf2_0, lnbuf2_0]
            ps_t2 = [pp(f"ps_t2_{i}", [128, 128], BF16) for i in range(4)]
            with nc.Block("C3") as blk:
                b_load, b_sp = s_load.n, s_spill.n
                b_tp, b_ev, b_ac, b_ve = s_tp.n, s_ev.n, s_ac.n, s_ve.n

                @blk.sync
                def _(sy):
                    for tc in range(8):
                        if tc >= 1:
                            sy.wait_ge(s_ve.h, b_ve + 4 * (tc - 1) + 3)
                        s_load.bump(sy.dma_start(attnb[tc % 2][:],
                                                 scr_attn2[tc * 128:(tc + 1) * 128, :]), 16)
                        s_load.bump(sy.dma_start(xorig[tc % 2][:],
                                                 x_in[3 + tc * 128:3 + (tc + 1) * 128, :]), 16)
                        sy.wait_ge(s_ve.h, b_ve + 4 * tc + 2)
                        s_spill.bump(sy.dma_start(scr_x12[tc * 128:(tc + 1) * 128, :],
                                                  x1b[tc % 2][:]), 16)

                @blk.vector
                def _(v):
                    for tc in range(8):
                        v.wait_ge(s_load.h, b_load + 32 * (tc + 1))
                        xa = attnb[tc % 2][:]
                        v.reduce_sum(st_s[:], xa, AX.X)
                        v.memset(st_ss[:], 0.0)
                        if tc >= 1:
                            v.wait_ge(s_spill.h, b_sp + 16 * tc)
                        v.scalar_tensor_tensor(x1b[tc % 2][:], xa, 1.0, xa, ALU.mult, ALU.mult,
                                               accum_out=st_ss[:])
                        v.tensor_scalar(st_m[:], st_s[:], 1.0 / DM, None, ALU.mult)
                        v.drain()
                        v.tensor_tensor(st_t[:], st_m[:], st_m[:], ALU.mult)
                        v.drain()
                        v.tensor_scalar(st_v[:], st_ss[:], 1.0 / DM, st_t[:],
                                        ALU.mult, ALU.subtract)
                        v.drain()
                        s_ve.bump(v.nop(), 1)
                        v.wait_ge(s_ac.h, b_ac + 2 * tc + 1)
                        v.tensor_scalar(x1b[tc % 2][:], xa, st_m[:], st_r[:],
                                        ALU.subtract, ALU.mult)
                        s_ve.bump(v.tensor_tensor(x1b[tc % 2][:], x1b[tc % 2][:],
                                                  xorig[tc % 2][:], ALU.add), 1)
                        xb = x1b[tc % 2][:]
                        v.reduce_sum(st_s[:], xb, AX.X)
                        v.memset(st_ss[:], 0.0)
                        v.scalar_tensor_tensor(attnb[tc % 2][:], xb, 1.0, xb, ALU.mult, ALU.mult,
                                               accum_out=st_ss[:])
                        v.tensor_scalar(st_m[:], st_s[:], 1.0 / DM, None, ALU.mult)
                        v.drain()
                        v.tensor_tensor(st_t[:], st_m[:], st_m[:], ALU.mult)
                        v.drain()
                        v.tensor_scalar(st_v[:], st_ss[:], 1.0 / DM, st_t[:],
                                        ALU.mult, ALU.subtract)
                        v.drain()
                        s_ve.bump(v.nop(), 1)
                        v.wait_ge(s_ac.h, b_ac + 2 * tc + 2)
                        if tc >= 1:
                            v.wait_ge(s_tp.h, b_tp + KC * tc)
                        s_ve.bump(v.tensor_scalar(lnbuf2[tc % 2][:], xb, st_m[:], st_r[:],
                                                  ALU.subtract, ALU.mult), 1)

                @blk.scalar
                def _(s):
                    for tc in range(8):
                        for ss in range(2):
                            s.wait_ge(s_ve.h, b_ve + 4 * tc + (1 if ss == 0 else 3))
                            s.activation(st_v[:], st_v[:], AF.Ln, bias=epscol[:])
                            s.drain()
                            s.activation(st_r[:], st_v[:], AF.Exp, scale=-0.5)
                            s.drain()
                            s_ac.bump(s.nop(), 1)
                        for k in range(KC):
                            s.wait_ge(s_tp.h, b_tp + tc * KC + k + 1)
                            s_ev.bump(s.copy(
                                ln1T[:, k * TH + 3 + tc * 128: k * TH + 3 + (tc + 1) * 128],
                                ps_t2[k % 4][:]), 1)

                @blk.tensor
                def _(te):
                    for tc in range(8):
                        te.wait_ge(s_ve.h, b_ve + 4 * tc + 4)
                        for k in range(KC):
                            idx = tc * KC + k
                            if idx >= 4:
                                te.wait_ge(s_ev.h, b_ev + idx - 3)
                            s_tp.bump(te.transpose(ps_t2[k % 4][:],
                                                   lnbuf2[tc % 2][:, k * 128:(k + 1) * 128],
                                                   identb[:]), 1)

        with contextlib.ExitStack() as pes:
            ls = lambda name, shape, dt: pes.enter_context(nc.sbuf_tensor(name, shape, dt))
            pp = lambda name, shape, dt: pes.enter_context(nc.psum_tensor(name, shape, dt))
            glbuf = ls("glbuf", [128, 8], F32)
            m1c = ls("m1c", [128, 1], F32)
            m2c = ls("m2c", [128, 1], F32)
            eq1 = ls("eq1", [128, 8], F32)
            eq2 = ls("eq2", [128, 8], F32)
            g2b = ls("g2b", [128, 8], F32)
            w1c = ls("w1c", [128, 1], F32)
            w2c = ls("w2c", [128, 1], F32)
            wfb = ls("wfb", [128, 8], BF16)
            wT = ls("wT", [8, 128], BF16)
            dpre_es = contextlib.ExitStack()
            pd = lambda name, shape, dt: dpre_es.enter_context(nc.psum_tensor(name, shape, dt))
            psg = pd("psg", [128, 8], F32)
            psw = pd("psw", [8, 128], BF16)
            psE = pd("psE", [128, 128], F32)
            psL2 = pd("psL2", [128, 512], F32)
            with nc.Block("Dpre") as blk:
                b_tp, b_ev, b_ac, b_ve = s_tp.n, s_ev.n, s_ac.n, s_ve.n
                @blk.tensor
                def _(te):
                    for tt in range(2):
                        mm = None
                        for k in range(KC):
                            mm = te.matmul(psL2[:], AallT[:, k * 128:(k + 1) * 128],
                                           ln1T[:, k * TH + 3 + tt * 512: k * TH + 3 + tt * 512 + 512],
                                           start=(k == 0), stop=(k == KC - 1))
                        s_tp.bump(mm, 1)
                        te.wait_ge(s_ev.h, b_ev + tt + 1)
                    for tc in range(8):
                        if tc >= 1:
                            te.wait_ge(s_ac.h, b_ac + 2 * (tc - 1) + 1)
                        mm = None
                        for k in range(KC):
                            mm = te.matmul(psg[:], ln1T[:, k * TH + 3 + tc * 128: k * TH + 3 + (tc + 1) * 128],
                                           gatewT[:, k * 8:(k + 1) * 8],
                                           start=(k == 0), stop=(k == KC - 1))
                        s_tp.bump(mm, 1)
                        te.wait_ge(s_ve.h, b_ve + 3 * tc + 2)
                        if tc >= 1:
                            te.wait_ge(s_ev.h, b_ev + 2 + tc)
                        s_tp.bump(te.transpose(psw[:], wfb[:], identb[:]), 1)
                        te.wait_ge(s_ev.h, b_ev + 2 + tc + 1)
                        if tc >= 1:
                            te.wait_ge(s_ve.h, b_ve + 3 * tc)
                        s_tp.bump(te.matmul(psE[:], emat[:], wT[:], start=True, stop=True), 1)

                @blk.scalar
                def _(s):
                    for tt in range(2):
                        s.wait_ge(s_tp.h, b_tp + tt + 1)
                        s_ev.bump(s.copy(lowT[:, tt * 512:(tt + 1) * 512], psL2[:]), 1)
                    for tc in range(8):
                        s.wait_ge(s_tp.h, b_tp + 2 + 3 * tc + 1)
                        s.copy(glbuf[:], psg[:])
                        s.drain()
                        s_ac.bump(s.nop(), 1)
                        s.wait_ge(s_ve.h, b_ve + 3 * tc + 1)
                        s.activation(w1c[:], m1c[:], AF.Sigmoid)
                        s.drain()
                        s_ac.bump(s.nop(), 1)
                        s.wait_ge(s_tp.h, b_tp + 2 + 3 * tc + 2)
                        s_ev.bump(s.copy(wT[:], psw[:]), 1)

                @blk.vector
                def _(v):
                    for tc in range(8):
                        v.wait_ge(s_ac.h, b_ac + 2 * tc + 1)
                        v.reduce_max(m1c[:], glbuf[:], AX.X)
                        v.drain()
                        v.tensor_scalar(eq1[:], glbuf[:], m1c[:], None, ALU.is_ge)
                        v.drain()
                        v.scalar_tensor_tensor(g2b[:], eq1[:], -1e30, glbuf[:], ALU.mult, ALU.add)
                        v.drain()
                        v.reduce_max(m2c[:], g2b[:], AX.X)
                        v.drain()
                        v.tensor_scalar(eq2[:], g2b[:], m2c[:], None, ALU.is_ge)
                        v.tensor_tensor(m1c[:], m1c[:], m2c[:], ALU.subtract)
                        v.drain()
                        s_ve.bump(v.nop(), 1)
                        v.wait_ge(s_ac.h, b_ac + 2 * tc + 2)

                        v.tensor_scalar(w1c[:], w1c[:], 2.0, None, ALU.mult)
                        v.drain()
                        v.tensor_scalar(w2c[:], w1c[:], -1.0, 2.0, ALU.mult, ALU.add)
                        v.drain()
                        v.tensor_scalar(eq1[:], eq1[:], w1c[:], None, ALU.mult)
                        v.tensor_scalar(eq2[:], eq2[:], w2c[:], None, ALU.mult)
                        v.drain()
                        v.tensor_tensor(wfb[:], eq1[:], eq2[:], ALU.add)
                        v.drain()
                        s_ve.bump(v.nop(), 1)
                        v.wait_ge(s_tp.h, b_tp + 2 + 3 * tc + 3)
                        v.tensor_tensor(lowwT[:, tc * 128:(tc + 1) * 128],
                                        lowT[:, tc * 128:(tc + 1) * 128], psE[:], ALU.mult)
                        v.drain()
                        s_ve.bump(v.nop(), 1)

            dpre_es.close()
            wbt = [ls(f"wbt{i}", [128, 512], BF16) for i in range(2)]
            fbuf = [ls(f"fbuf{i}", [128, 512], BF16) for i in range(2)]
            psF = [pp(f"psF{i}", [128, 512], F32) for i in range(8)]
            wb2 = wt_base.rearrange("(k p) c -> p k c", p=128)
            with nc.Block("Dmain") as blk:
                b_cast, b_sp = s_cast.n, s_spill.n
                b_tp, b_ev = s_tp.n, s_ev.n

                @blk.gpsimd
                def _(g):
                    for cht in range(8):
                        for k in range(KC):
                            q = cht * KC + k
                            if q >= 2:
                                qq = q - 2
                                g.wait_ge(s_tp.h, b_tp + (qq // KC) * (KC + 1) + (qq % KC) + 1)
                            s_cast.bump(g.dma_start(wbt[q % 2][:],
                                                    wb2[:, k, cht * 512:(cht + 1) * 512]), 16)

                @blk.tensor
                def _(te):
                    for cht in range(8):
                        if cht >= 1:
                            te.wait_ge(s_ev.h, b_ev + 8 * cht)
                        for k in range(KC):
                            q = cht * KC + k
                            te.wait_ge(s_cast.h, b_cast + 16 * (q + 1))
                            mmk = None
                            for tc in range(8):
                                mmk = te.matmul(psF[tc][:],
                                                ln1T[:, k * TH + 3 + tc * 128: k * TH + 3 + (tc + 1) * 128],
                                                wbt[q % 2][:], start=(k == 0), stop=False)
                            s_tp.bump(mmk, 1)
                        mm = None
                        for tc in range(8):
                            mm = te.matmul(psF[tc][:], lowwT[:, tc * 128:(tc + 1) * 128],
                                           BallT[:, cht * 512:(cht + 1) * 512],
                                           start=False, stop=False)
                            mm = te.matmul(psF[tc][:], ones1[:],
                                           bbrow[0:1, cht * 512:(cht + 1) * 512],
                                           start=False, stop=True)
                        s_tp.bump(mm, 1)

                @blk.scalar
                def _(s):
                    for cht in range(8):
                        s.wait_ge(s_tp.h, b_tp + (cht + 1) * (KC + 1))
                        for tc in range(8):
                            e = cht * 8 + tc
                            if e >= 2:
                                s.wait_ge(s_spill.h, b_sp + 16 * (e - 1))
                            s_ev.bump(s.copy(fbuf[e % 2][:], psF[tc][:]), 1)

                @blk.sync
                def _(sy):
                    for cht in range(8):
                        for tc in range(8):
                            e = cht * 8 + tc
                            sy.wait_ge(s_ev.h, b_ev + e + 1)
                            s_spill.bump(sy.dma_start(
                                scr_ffn[tc * 128:(tc + 1) * 128, cht * 512:(cht + 1) * 512],
                                fbuf[e % 2][:]), 16)

            ffb = [ls(f"ffb{i}", [128, DM], BF16) for i in range(2)]
            x1c0 = ls("x1c0", [128, DM], F32)
            x1c = [x1c0, x1c0]
            outb0 = ls("outb0", [128, DM], F32)
            outb = [outb0, outb0]
            with nc.Block("D2") as blk:
                b_load, b_sp = s_load.n, s_spill.n
                b_ac, b_ve = s_ac.n, s_ve.n

                @blk.sync
                def _(sy):
                    for tc in range(8):
                        if tc >= 2:
                            sy.wait_ge(s_ve.h, b_ve + 2 * (tc - 1))
                        s_load.bump(sy.dma_start(ffb[tc % 2][:],
                                                 scr_ffn[tc * 128:(tc + 1) * 128, :]), 16)
                        if tc >= 1:
                            sy.wait_ge(s_ve.h, b_ve + 2 * tc)
                        s_load.bump(sy.dma_start(x1c[tc % 2][:],
                                                 scr_x12[tc * 128:(tc + 1) * 128, :]), 16)
                        sy.wait_ge(s_ve.h, b_ve + 2 * tc + 2)
                        s_out.bump(sy.dma_start(out[tc * 128:(tc + 1) * 128, :],
                                                outb[tc % 2][:]), 16)

                @blk.vector
                def _(v):
                    for tc in range(8):
                        v.wait_ge(s_load.h, b_load + 32 * (tc + 1))
                        xa = ffb[tc % 2][:]
                        v.reduce_sum(st_s[:], xa, AX.X)
                        v.memset(st_ss[:], 0.0)
                        if tc >= 1:
                            v.wait_ge(s_out.h, 16 * tc)
                        v.scalar_tensor_tensor(outb[tc % 2][:], xa, 1.0, xa, ALU.mult, ALU.mult,
                                               accum_out=st_ss[:])
                        v.tensor_scalar(st_m[:], st_s[:], 1.0 / DM, None, ALU.mult)
                        v.drain()
                        v.tensor_tensor(st_t[:], st_m[:], st_m[:], ALU.mult)
                        v.drain()
                        v.tensor_scalar(st_v[:], st_ss[:], 1.0 / DM, st_t[:],
                                        ALU.mult, ALU.subtract)
                        v.drain()
                        s_ve.bump(v.nop(), 1)
                        v.wait_ge(s_ac.h, b_ac + tc + 1)
                        v.tensor_scalar(outb[tc % 2][:], xa, st_m[:], st_r[:],
                                        ALU.subtract, ALU.mult)
                        s_ve.bump(v.tensor_tensor(outb[tc % 2][:], outb[tc % 2][:],
                                                  x1c[tc % 2][:], ALU.add), 1)

                @blk.scalar
                def _(s):
                    for tc in range(8):
                        s.wait_ge(s_ve.h, b_ve + 2 * tc + 1)
                        s.activation(st_v[:], st_v[:], AF.Ln, bias=epscol[:])
                        s.drain()
                        s.activation(st_r[:], st_v[:], AF.Exp, scale=-0.5)
                        s.drain()
                        s_ac.bump(s.nop(), 1)

        with nc.Block("fin") as blk:
            @blk.gpsimd
            def _(g):
                g.wait_ge(s_out.h, s_out.n)
            @blk.sync
            def _(sy):
                sy.wait_ge(s_spill.h, s_spill.n)

    if part != 1:
        emit_part2()
        es.close()
        return nc

    with nc.Block("init") as blk:
        b_load = s_load.n

        @blk.gpsimd
        def _(g):
            g.engine_nop()

        @blk.sync
        def _(sy):
            for dst, src in [(identb, ident_in), (identf, identf_in), (maskl, maskl_in),
                             (maskr, maskr_in), (cw, cw_in), (cb, cb_in), (dtb, dtb_in),
                             (negA, alog_in), (m8, m8_in), (m8c, m8c_in), (emat, emat_in)]:
                s_load.bump(sy.dma_start(dst[:], src), 16)

        @blk.scalar
        def _(s):
            s.wait_ge(s_load.h, b_load + 176)
            s.activation(negA[:], negA[:], AF.Exp)
            s.drain()
            s_ac.bump(s.nop(), 1)

        @blk.vector
        def _(v):
            v.memset(ones1[:], 1.0)
            v.memset(zcol[:], 0.0)
            v.memset(epscol[:], EPS)
            v.memset(zbig[:], 0.0)
            v.memset(Rst[:], 0.0)
            v.wait_ge(s_ac.h, s_ac.n)
            v.tensor_scalar(negA[:], negA[:], -1.0, None, ALU.mult)


    def emit_ln_stats(v, xa, nr):
        Fdim = xa.shape[-1]
        v.reduce_sum(st_s[0:nr], xa, AX.X)
        v.memset(st_ss[0:nr], 0.0)
        v.scalar_tensor_tensor(sq_dump[0:nr, 0:Fdim], xa, 1.0, xa, ALU.mult, ALU.mult,
                               accum_out=st_ss[0:nr])
        v.tensor_scalar(st_m[0:nr], st_s[0:nr], 1.0 / Fdim, None, ALU.mult)
        v.drain()
        v.tensor_tensor(st_t[0:nr], st_m[0:nr], st_m[0:nr], ALU.mult)
        v.drain()
        v.tensor_scalar(st_v[0:nr], st_ss[0:nr], 1.0 / Fdim, st_t[0:nr],
                        ALU.mult, ALU.subtract)
        v.drain()
        return v.nop()

    def emit_rstd(s, nr):
        s.activation(st_v[0:nr], st_v[0:nr], AF.Ln, bias=epscol[0:nr])
        s.drain()
        s.activation(st_r[0:nr], st_v[0:nr], AF.Exp, scale=-0.5)
        s.drain()
        return s.nop()

    with contextlib.ExitStack() as pes:
        ls = lambda name, shape, dt: pes.enter_context(nc.sbuf_tensor(name, shape, dt))
        pp = lambda name, shape, dt: pes.enter_context(nc.psum_tensor(name, shape, dt))
        xbuf = [ls(f"xbuf{i}", [128, DM], F32) for i in range(2)]
        lnbuf = [ls(f"lnbuf{i}", [128, DM], BF16) for i in range(2)]
        sq_dump = ls("sq_dump", [128, DM], F32)
        ps_t = [pp(f"ps_t{i}", [128, 128], BF16) for i in range(4)]
        GROUPS = [(0, 3)] + [(3 + 128 * i, 128) for i in range(8)]

        with nc.Block("A0") as blk:
            b_ax, b_av, b_ar, b_an, b_tp, b_ev2 = (s_load.n, s_ac.n, s_aux.n,
                                                   s_ve.n, s_tp.n, s_ev.n)

            @blk.sync
            def _(sy):
                for gi, (r0, nr) in enumerate(GROUPS):
                    if gi >= 2:
                        sy.wait_ge(s_ve.h, b_an + gi - 1)
                    s_load.bump(sy.dma_start(xbuf[gi % 2][0:nr, :], x_in[r0:r0 + nr, :]), 16)

            @blk.vector
            def _(v):
                for gi, (r0, nr) in enumerate(GROUPS):
                    v.wait_ge(s_load.h, b_ax + 16 * (gi + 1))
                    xa = xbuf[gi % 2][0:nr, :]
                    s_ac.bump(emit_ln_stats(v, xa, nr), 1)
                    v.wait_ge(s_aux.h, b_ar + gi + 1)
                    if gi >= 2:
                        v.wait_ge(s_tp.h, b_tp + KC * (gi - 1))
                    s_ve.bump(v.tensor_scalar(lnbuf[gi % 2][0:nr, :], xa, st_m[0:nr],
                                              st_r[0:nr], ALU.subtract, ALU.mult), 1)

            @blk.scalar
            def _(s):
                for gi, (r0, nr) in enumerate(GROUPS):
                    s.wait_ge(s_ac.h, b_av + gi + 1)
                    s_aux.bump(emit_rstd(s, nr), 1)
                    for k in range(KC):
                        s.wait_ge(s_tp.h, b_tp + gi * KC + k + 1)
                        s_ev.bump(s.copy(ln1T[:, k * TH + r0: k * TH + r0 + nr],
                                         ps_t[k % 4][:, 0:nr]), 1)

            @blk.tensor
            def _(te):
                for gi, (r0, nr) in enumerate(GROUPS):
                    te.wait_ge(s_ve.h, b_an + gi + 1)
                    for k in range(KC):
                        idx = gi * KC + k
                        if idx >= 4:
                            te.wait_ge(s_ev.h, b_ev2 + idx - 3)
                        s_tp.bump(te.transpose(ps_t[k % 4][:, 0:nr],
                                               lnbuf[gi % 2][0:nr, k * 128:(k + 1) * 128],
                                               identb[0:nr, 0:nr]), 1)

    with contextlib.ExitStack() as pes:
        ls = lambda name, shape, dt: pes.enter_context(nc.sbuf_tensor(name, shape, dt))
        pp = lambda name, shape, dt: pes.enter_context(nc.psum_tensor(name, shape, dt))
        wbufA = [ls(f"wbufA{i}", [128, KC * 128], BF16) for i in range(2)]
        xbcb = [ls(f"xbcb{i}", [128, TH], F32) for i in range(2)]
        accb = [ls(f"accb{i}", [128, T], F32) for i in range(2)]
        xoutb = [ls(f"xoutb{i}", [128, T], BF16) for i in range(2)]
        psA512 = [pp(f"psA512_{i}", [128, 512], F32) for i in range(4)]
        psA3 = [pp(f"psA3_{i}", [128, 3], F32) for i in range(2)]
        NT = 81
        wsrc = wt_xbcdt.rearrange("(k p) c -> p k c", p=128)

        with nc.Block("A1") as blk:
            b_cast, b_pe, b_ev, b_ve, b_ac, b_sp = (s_cast.n, s_pe.n, s_ev.n,
                                                    s_ve.n, s_ac.n, s_spill.n)

            @blk.gpsimd
            def _(g):
                for ct in range(NT):
                    if ct >= 2:
                        g.wait_ge(s_pe.h, b_pe + ct - 1)
                    s_cast.bump(g.dma_start(
                        wbufA[ct % 2][:].rearrange("p (k m) -> p k m", k=KC),
                        wsrc[:, :, ct * 128:(ct + 1) * 128]), 16)

            @blk.tensor
            def _(te):
                for ct in range(NT):
                    te.wait_ge(s_cast.h, b_cast + 16 * (ct + 1))
                    if ct >= 2:
                        te.wait_ge(s_ev.h, b_ev + ct - 1)
                    pa = (psA512[2 * (ct % 2)], psA512[2 * (ct % 2) + 1], psA3[ct % 2])
                    mm = None
                    for k in range(KC):
                        lt = wbufA[ct % 2][:, k * 128:(k + 1) * 128]
                        st_ = (k == 0)
                        sp_ = (k == KC - 1)
                        te.matmul(pa[0][:], lt, ln1T[:, k * TH: k * TH + 512],
                                  start=st_, stop=sp_)
                        te.matmul(pa[1][:], lt, ln1T[:, k * TH + 512: k * TH + 1024],
                                  start=st_, stop=sp_)
                        mm = te.matmul(pa[2][:], lt, ln1T[:, k * TH + 1024: k * TH + 1027],
                                       start=st_, stop=sp_)
                    s_pe.bump(mm, 1)

            @blk.scalar
            def _(s):
                for ct in range(NT + 1):
                    if ct <= NT - 1:
                        s.wait_ge(s_pe.h, b_pe + ct + 1)
                        if ct >= 2:
                            s.wait_ge(s_ve.h, b_ve + ct - 1)
                        pa = (psA512[2 * (ct % 2)], psA512[2 * (ct % 2) + 1], psA3[ct % 2])
                        xb = xbcb[ct % 2]
                        s.copy(xb[:, 0:512], pa[0][:])
                        s.copy(xb[:, 512:1024], pa[1][:])
                        s_ev.bump(s.copy(xb[:, 1024:1027], pa[2][:]), 1)
                    if 1 <= ct <= NT - 1:
                        s.wait_ge(s_ve.h, b_ve + ct)
                        if ct - 1 >= 2:
                            s.wait_ge(s_spill.h, b_sp + 16 * (ct - 2))
                        s_ac.bump(s.activation(xoutb[(ct - 1) % 2][:], accb[(ct - 1) % 2][:],
                                               AF.Silu, bias=cb[:, ct - 1:ct]), 1)
                s.wait_ge(s_ve.h, b_ve + NT)
                s.activation(accb[0][:], dt_fm[:], AF.Abs)
                s.activation(accb[0][:], accb[0][:], AF.Exp, scale=-1.0)
                s_ac.bump(s.activation(accb[0][:], accb[0][:], AF.Ln, bias=1.0), 1)

            @blk.vector
            def _(v):
                for ct in range(NT):
                    v.wait_ge(s_ev.h, b_ev + ct + 1)
                    xb = xbcb[ct % 2]
                    if ct <= 79:
                        acc = accb[ct % 2]
                        if ct >= 2:
                            v.wait_ge(s_ac.h, b_ac + ct - 1)
                        v.tensor_scalar(acc[:], xb[:, 0:1024], cw[:, ct * 4: ct * 4 + 1],
                                        None, ALU.mult)
                        for jj in (1, 2, 3):
                            v.scalar_tensor_tensor(acc[:], xb[:, jj:jj + 1024],
                                                   cw[:, ct * 4 + jj: ct * 4 + jj + 1],
                                                   acc[:], ALU.mult, ALU.add)
                        s_ve.bump(v.engine_nop(), 1)
                    else:
                        s_ve.bump(v.tensor_scalar(dt_fm[:], xb[:, 3:1027], dtb[:], None,
                                                  ALU.add), 1)
                v.wait_ge(s_ac.h, b_ac + NT)
                v.tensor_scalar(accb[1][:], dt_fm[:], 0.0, None, ALU.max)
                v.tensor_tensor(dt_fm[:], accb[1][:], accb[0][:], ALU.add)
                v.tensor_scalar(accb[0][:], dt_fm[:], negA[:], None, ALU.mult)
                s_ve.bump(v.tensor_tensor_scan(cumAH[:], accb[0][:], zbig[:], 0.0,
                                               ALU.add, ALU.add), 1)

            @blk.sync
            def _(sy):
                for j in range(80):
                    sy.wait_ge(s_ac.h, b_ac + j + 1)
                    s_spill.bump(sy.dma_start(scr_xsilu[j * 128:(j + 1) * 128, :],
                                              xoutb[j % 2][:]), 16)
                if debug:
                    sy.wait_ge(s_ve.h, s_ve.n)
                    s_spill.bump(sy.dma_start(dbg_dt, dt_fm[:]), 16)
                    s_spill.bump(sy.dma_start(dbg_cum, cumAH[:]), 16)
                    s_spill.bump(sy.dma_start(dbg_ln, ln1T[:, 0:2 * TH]), 16)


    with contextlib.ExitStack() as pes:
        ls = lambda name, shape, dt: pes.enter_context(nc.sbuf_tensor(name, shape, dt))
        pp = lambda name, shape, dt: pes.enter_context(nc.psum_tensor(name, shape, dt))
        whalf = [ls(f"whalf{i}", [128, 16 * 512], BF16) for i in range(3)]
        gbuf = [ls(f"gbuf{i}", [128, 512], BF16) for i in range(2)]
        psB = [pp(f"psB{i}", [128, 512], F32) for i in range(8)]
        wsrc2 = wt_gate.rearrange("(k p) c -> p k c", p=128)

        with nc.Block("B") as blk:
            b_cast, b_pe, b_ev, b_sp = s_cast.n, s_pe.n, s_ev.n, s_spill.n

            @blk.gpsimd
            def _(g):
                for ct in range(16):
                    for half in range(2):
                        q = 2 * ct + half
                        if q >= 3:
                            g.wait_ge(s_pe.h, b_pe + q - 2)
                        s_cast.bump(g.dma_start(
                            whalf[q % 3][:].rearrange("p (kl c) -> p kl c", kl=16),
                            wsrc2[:, 16 * half:16 * half + 16, ct * 512:(ct + 1) * 512]), 16)

            @blk.tensor
            def _(te):
                for ct in range(16):
                    for half in range(2):
                        q = 2 * ct + half
                        te.wait_ge(s_cast.h, b_cast + 16 * (q + 1))
                        if half == 0 and ct >= 1:
                            te.wait_ge(s_ev.h, b_ev + 8 * ct)
                        mm = None
                        for kl in range(16):
                            kk = 16 * half + kl
                            for tc in range(8):
                                mm = te.matmul(psB[tc][:],
                                               ln1T[:, kk * TH + 3 + tc * 128: kk * TH + 3 + (tc + 1) * 128],
                                               whalf[q % 3][:, kl * 512:(kl + 1) * 512],
                                               start=(half == 0 and kl == 0),
                                               stop=(half == 1 and kl == 15))
                        s_pe.bump(mm, 1)

            @blk.scalar
            def _(s):
                for ct in range(16):
                    s.wait_ge(s_pe.h, b_pe + 2 * (ct + 1))
                    for tc in range(8):
                        e = 8 * ct + tc
                        if e >= 2:
                            s.wait_ge(s_spill.h, b_sp + 16 * (e - 1))
                        s_ev.bump(s.copy(gbuf[e % 2][:], psB[tc][:]), 1)

            @blk.sync
            def _(sy):
                for ct in range(16):
                    for tc in range(8):
                        e = 8 * ct + tc
                        sy.wait_ge(s_ev.h, b_ev + e + 1)
                        s_spill.bump(sy.dma_start(
                            scr_gate[tc * 128:(tc + 1) * 128, ct * 512:(ct + 1) * 512],
                            gbuf[e % 2][:]), 16)

    with contextlib.ExitStack() as pes:
        ls = lambda name, shape, dt: pes.enter_context(nc.sbuf_tensor(name, shape, dt))
        pp = lambda name, shape, dt: pes.enter_context(nc.psum_tensor(name, shape, dt))
        x_fm = ls("x_fm", [128, DIN], BF16)
        x_tok = ls("x_tok", [128, DIN], BF16)
        xsdbuf = ls("xsdbuf", [128, DIN], BF16)
        Bbuf = ls("Bbuf", [128, 1024], BF16)
        Btok = ls("Btok", [128, 1024], BF16)
        S_buf = ls("S_buf", [128, DIN], BF16)
        D_wide = ls("D_wide", [128, DIN], BF16)
        drow = ls("drow", [1, 128], BF16)
        dtT_sb = ls("dtT_sb", [128, 128], F32)
        dscT_sb = ls("dscT_sb", [128, 128], F32)
        decayH = ls("decayH", [128, 128], F32)
        dsc = ls("dsc", [128, 128], F32)
        sdecH = ls("sdecH", [128, 128], F32)
        nbcol = ls("nbcol", [128, 1], F32)
        cdtmp = ls("cdtmp", [128, 1], F32)
        cdcol = ls("cdcol", [128, 1], F32)
        dccol = ls("dccol", [128, 1], F32)
        psTf = [pp(f"psTf{i}", [128, 128], F32) for i in range(2)]
        psT2 = [pp(f"psT2_{i}", [128, 128], BF16) for i in range(2)]
        psS = [pp(f"psS{i}", [128, 512], F32) for i in range(2)]
        psC = pp("psC", [128, 128], F32)

        with nc.Block("A2a_dw") as blk:
            b_cast, b_pe, b_ev = s_cast.n, s_pe.n, s_ev.n

            @blk.gpsimd
            def _(g):
                s_cast.bump(g.dma_start(drow[:], drow_in), 16)

            @blk.tensor
            def _(te):
                te.wait_ge(s_cast.h, b_cast + 16)
                for i in range(16):
                    if i >= 2:
                        te.wait_ge(s_ev.h, b_ev + i - 1)
                    rhs = drow[0:1, i * 8:(i + 1) * 8].unsqueeze(2).broadcast_to([1, 8, 64])
                    s_pe.bump(te.matmul(psS[i % 2][:], ones1[:], rhs, start=True, stop=True), 1)

            @blk.scalar
            def _(s):
                for i in range(16):
                    s.wait_ge(s_pe.h, b_pe + i + 1)
                    s_ev.bump(s.copy(D_wide[:, i * 512:(i + 1) * 512], psS[i % 2][:]), 1)

        with nc.Block("A2a") as blk:
            b_load, b_cast, b_sp = s_load.n, s_cast.n, s_spill.n
            b_tp, b_ev, b_ac, b_ve = s_tp.n, s_ev.n, s_ac.n, s_ve.n
            PL, PC, PS = 64, 16, 48
            PT, PE_, PA, PV = 92, 91, 3, 7
            xsrc = scr_xsilu.rearrange("(a p) t -> p a t", p=128)

            @blk.sync
            def _(sy):
                for c in range(NCH):
                    if c >= 1:
                        sy.wait_ge(s_spill.h, b_sp + PS * (c - 1) + 16)
                    s_load.bump(sy.dma_start(
                        x_fm[:].rearrange("p (a t) -> p a t", a=64),
                        xsrc[:, 0:64, c * 128:(c + 1) * 128]), 16)
                    if c >= 1:
                        sy.wait_ge(s_tp.h, b_tp + PT * (c - 1) + 74)
                    s_load.bump(sy.dma_start(
                        Bbuf[:].rearrange("p (a t) -> p a t", a=8),
                        xsrc[:, 64:72, c * 128:(c + 1) * 128]), 16)
                    ctdst = cT_all[:].rearrange("p (g c t) -> p g c t", g=8, c=NCH)
                    s_load.bump(sy.dma_start(
                        ctdst[:, :, c, :], xsrc[:, 72:80, c * 128:(c + 1) * 128]), 16)
                    sy.wait_ge(s_ve.h, b_ve + PV * c + 3)
                    s_spill.bump(sy.dma_start(scr_xs[c * 128:(c + 1) * 128, :], x_fm[:]), 16)
                    sy.wait_ge(s_ev.h, b_ev + PE_ * c + 90)
                    s_spill.bump(sy.dma_start(scr_S[c * 128:(c + 1) * 128, :], S_buf[:]), 16)
                    sy.wait_ge(s_ve.h, b_ve + PV * c + 6)
                    s_spill.bump(sy.dma_start(scr_Ypart[c * 128:(c + 1) * 128, :], xsdbuf[:]), 16)
                    sy.wait_ge(s_cast.h, b_cast + PC * (c + 1))
                    s_load.bump(sy.dma_start(cdrow_bf[0:1, c * 128:(c + 1) * 128],
                                             scr_cd[c:c + 1, :]), 16)
                sy.wait_ge(s_ve.h, b_ve + PV * NCH)
                s_spill.bump(sy.dma_start(cc_in[0:128, :], Rst[:]), 16)
                s_spill.bump(sy.dma_start(xp_cT, cT_all[:]), 16)
                s_spill.bump(sy.dma_start(xp_sdecT, sdecT[:]), 16)
                s_spill.bump(sy.dma_start(xp_cdrow, cdrow_bf[:]), 16)
                if debug:
                    s_spill.bump(sy.dma_start(dbg_R, Rst[:]), 16)

            @blk.gpsimd
            def _(g):
                for c in range(NCH):
                    g.wait_ge(s_ac.h, b_ac + PA * c + 3)
                    s_cast.bump(g.dma_start(scr_cd[c:c + 1, :].rearrange("a b -> (a b)").unsqueeze(1),
                                            cdcol[:]), 16)
                g.wait_ge(s_ac.h, b_ac + PA * NCH + 1)
                s_cast.bump(g.dma_start(cc_in[128:129, 0:128].rearrange("a b -> (a b)").unsqueeze(1),
                                        dccol[:]), 16)
                g.wait_ge(s_spill.h, b_sp + PS * NCH + 16)
                g.wait_ge(s_cast.h, b_cast + PC * NCH + 16)
                s_cc.bump(g.nop(), 16)

            @blk.tensor
            def _(te):
                for c in range(NCH):
                    base_e = b_ev + PE_ * c
                    base_v = b_ve + PV * c
                    if c >= 1:
                        te.wait_ge(s_ev.h, b_ev + PE_ * (c - 1) + 2)
                    s_tp.bump(te.transpose(psTf[0][:], dt_fm[:, c * 128:(c + 1) * 128], identf[:]), 1)
                    te.wait_ge(s_ve.h, base_v + 2)
                    s_tp.bump(te.transpose(psTf[1][:], dsc[:], identf[:]), 1)
                    te.wait_ge(s_load.h, b_load + PL * c + 16)
                    if c >= 1:
                        te.wait_ge(s_ev.h, b_ev + PE_ * (c - 1) + 74)
                    for i in range(64):
                        if i >= 2:
                            te.wait_ge(s_ev.h, base_e + 2 + i - 1)
                        s_tp.bump(te.transpose(psT2[i % 2][:],
                                               x_fm[:, i * 128:(i + 1) * 128], identb[:]), 1)
                    te.wait_ge(s_load.h, b_load + PL * c + 32)
                    for i in range(64, 72):
                        te.wait_ge(s_ev.h, base_e + 2 + i - 1)
                        s_tp.bump(te.transpose(psT2[i % 2][:],
                                               Bbuf[:, (i - 64) * 128:(i - 63) * 128], identb[:]), 1)
                    te.wait_ge(s_ve.h, base_v + 4)
                    te.wait_ge(s_ev.h, base_e + 74)
                    for m in range(16):
                        gidx = m // 2
                        hf = m % 2
                        if m >= 2:
                            te.wait_ge(s_ev.h, base_e + 74 + m - 1)
                        elif c >= 1:
                            te.wait_ge(s_ev.h, b_ev + PE_ * (c - 1) + 90)
                        s_tp.bump(te.matmul(psS[m % 2][:],
                                            Btok[:, gidx * 128:(gidx + 1) * 128],
                                            xsdbuf[:, (gidx * 1024 + hf * 512):(gidx * 1024 + hf * 512 + 512)],
                                            start=True, stop=True), 1)
                    te.wait_ge(s_ac.h, b_ac + PA * c + 2)
                    te.wait_ge(s_ev.h, base_e + 2)
                    s_tp.bump(te.transpose(psTf[0][:], sdecH[:], identf[:]), 1)
                    te.wait_ge(s_load.h, b_load + PL * c + 64)
                    if c >= 1:
                        te.wait_ge(s_ve.h, b_ve + PV * (c - 1) + 7)
                    s_tp.bump(te.matmul(psC[:], ones1[:], cdrow_bf[0:1, c * 128:(c + 1) * 128],
                                        start=True, stop=True), 1)

            @blk.scalar
            def _(s):
                for c in range(NCH):
                    base_t = b_tp + PT * c
                    base_v = b_ve + PV * c
                    s_ac.bump(s.activation(decayH[:], cumAH[:, c * 128:(c + 1) * 128], AF.Exp,
                                           bias=cumAH[:, c * 128 + 127:c * 128 + 128],
                                           scale=-1.0), 1)
                    s.wait_ge(s_ve.h, base_v + 1)
                    s_ac.bump(s.activation(sdecH[:], cumAH[:, c * 128:(c + 1) * 128], AF.Exp,
                                           bias=nbcol[:]), 1)
                    s.wait_ge(s_tp.h, base_t + 1)
                    s_ev.bump(s.copy(dtT_sb[:], psTf[0][:]), 1)
                    s.wait_ge(s_tp.h, base_t + 2)
                    s_ev.bump(s.copy(dscT_sb[:], psTf[1][:]), 1)
                    for i in range(64):
                        s.wait_ge(s_tp.h, base_t + 2 + i + 1)
                        s_ev.bump(s.copy(x_tok[:, i * 128:(i + 1) * 128], psT2[i % 2][:]), 1)
                    for i in range(64, 72):
                        s.wait_ge(s_tp.h, base_t + 2 + i + 1)
                        s_ev.bump(s.copy(Btok[:, (i - 64) * 128:(i - 63) * 128], psT2[i % 2][:]), 1)
                    if c >= 1:
                        s.wait_ge(s_spill.h, b_sp + PS * (c - 1) + 32)
                        s.wait_ge(s_ve.h, b_ve + PV * (c - 1) + 7)
                    for m in range(16):
                        s.wait_ge(s_tp.h, base_t + 74 + m + 1)
                        s_ev.bump(s.copy(S_buf[:, m * 512:(m + 1) * 512], psS[m % 2][:]), 1)
                    s.wait_ge(s_tp.h, base_t + 91)
                    s_ev.bump(s.copy(sdecT[:, c * 128:(c + 1) * 128], psTf[0][:]), 1)
                    s.wait_ge(s_ve.h, base_v + 5)
                    s.activation(cdcol[:], cdtmp[:], AF.Exp)
                    s.drain()
                    s_ac.bump(s.nop(), 1)
                s.activation(dccol[:], cumAH[:, T - 1:T], AF.Exp)
                s.drain()
                s_ac.bump(s.nop(), 1)

            @blk.vector
            def _(v):
                for c in range(NCH):
                    base_e = b_ev + PE_ * c
                    base_t = b_tp + PT * c
                    base_a = b_ac + PA * c
                    basecol = zcol[:] if c == 0 else cumAH[:, c * 128 - 1:c * 128]
                    v.tensor_scalar(nbcol[:], basecol, -1.0, None, ALU.mult)
                    v.drain()
                    s_ve.bump(v.nop(), 1)
                    v.wait_ge(s_ac.h, base_a + 1)
                    s_ve.bump(v.tensor_tensor(dsc[:], dt_fm[:, c * 128:(c + 1) * 128],
                                              decayH[:], ALU.mult), 1)
                    v.wait_ge(s_ev.h, base_e + 74)
                    dupdt = dtT_sb[:].unsqueeze(2).broadcast_to([128, 128, 64])
                    s_ve.bump(v.tensor_tensor(x_fm[:].rearrange("p (h d) -> p h d", h=128),
                                              x_tok[:].rearrange("p (h d) -> p h d", h=128),
                                              dupdt, ALU.mult), 1)
                    if c >= 1:
                        v.wait_ge(s_spill.h, b_sp + PS * (c - 1) + 48)
                    dupds = dscT_sb[:].unsqueeze(2).broadcast_to([128, 128, 64])
                    s_ve.bump(v.tensor_tensor(xsdbuf[:].rearrange("p (h d) -> p h d", h=128),
                                              x_tok[:].rearrange("p (h d) -> p h d", h=128),
                                              dupds, ALU.mult), 1)
                    v.tensor_tensor(cdtmp[:], cumAH[:, c * 128 + 127:c * 128 + 128],
                                    nbcol[:], ALU.add)
                    v.drain()
                    s_ve.bump(v.nop(), 1)
                    v.wait_ge(s_tp.h, base_t + 90)
                    s_ve.bump(v.tensor_tensor(xsdbuf[:], x_tok[:], D_wide[:], ALU.mult), 1)
                    v.wait_ge(s_tp.h, base_t + 92)
                    v.wait_ge(s_ev.h, base_e + 90)
                    duppc = psC[:].unsqueeze(2).broadcast_to([128, 128, 64])
                    v.tensor_tensor(Rst[:].rearrange("p (h d) -> p h d", h=128),
                                    Rst[:].rearrange("p (h d) -> p h d", h=128), duppc, ALU.mult)
                    s_ve.bump(v.tensor_tensor(Rst[:], Rst[:], S_buf[:], ALU.add), 1)


    with contextlib.ExitStack() as pes:
        ls = lambda name, shape, dt: pes.enter_context(nc.sbuf_tensor(name, shape, dt))
        pp = lambda name, shape, dt: pes.enter_context(nc.psum_tensor(name, shape, dt))
        xsb = ls("xsb", [128, DIN], BF16)
        xfm2 = ls("xfm2", [128, DIN], BF16)
        Bbuf2 = ls("Bbuf2", [128, 1024], BF16)
        Ypin = ls("Ypin", [128, DIN], BF16)
        GmT_sb = ls("GmT_sb", [128, 1024], BF16)
        ebuf = [ls(f"ebuf{i}", [128, 512], BF16) for i in range(2)]
        Zbuf = [ls(f"Zbuf{i}", [128, 512], BF16) for i in range(2)]
        DdiagSB = ls("DdiagSB", [128, KD * 128], BF16)
        Lwin = ls("Lwin", [4, 8192], BF16)
        crebuf = ls("crebuf", [128, 128], F32)
        hi_bf = ls("hi_bf", [128, 128], BF16)
        lo_bf = ls("lo_bf", [128, 128], BF16)
        nhi_bf = ls("nhi_bf", [128, 128], BF16)
        nlo_bf = ls("nlo_bf", [128, 128], BF16)
        psG = pp("psG", [128, 128], F32)
        psL = [pp(f"psL{i}", [128, 512], F32) for i in range(2)]
        psY = [pp(f"psY{i}", [128, 512], F32) for i in range(2)]
        xsrc2 = scr_xsilu.rearrange("(a p) t -> p a t", p=128)

        with nc.Block("A2b_init") as blk:
            b_cast = s_cast.n

            @blk.gpsimd
            def _(g):
                s_cast.bump(g.dma_start(DdiagSB[:].rearrange("p (k c) -> p k c", k=KD),
                                        ddiag_in.rearrange("(k p) c -> p k c", p=128)), 16)

            @blk.vector
            def _(v):
                v.memset(Lwin[:], 1.0)
                v.wait_ge(s_cast.h, b_cast + 16)

        with nc.Block("A2b") as blk:
            b_load, b_sp = s_load.n, s_spill.n
            b_tp, b_ev, b_ve = s_tp.n, s_ev.n, s_ve.n
            PL, PS = 320, 80
            PT, PE_, PV = 56, 40, 49
            flat_w = [0, 1, 2, 3]

            @blk.sync
            def _(sy):
                for c in range(NCH):
                    if c >= 1:
                        sy.wait_ge(s_tp.h, b_tp + PT * c)
                        sy.wait_ge(s_ve.h, b_ve + PV * c)
                        sy.wait_ge(s_spill.h, b_sp + PS * c)
                    s_load.bump(sy.dma_start(xsb[:], scr_xs[c * 128:(c + 1) * 128, :]), 16)
                    s_load.bump(sy.dma_start(
                        xfm2[:].rearrange("p (a t) -> p a t", a=64),
                        xsrc2[:, 0:64, c * 128:(c + 1) * 128]), 16)
                    s_load.bump(sy.dma_start(
                        Bbuf2[:].rearrange("p (a t) -> p a t", a=8),
                        xsrc2[:, 64:72, c * 128:(c + 1) * 128]), 16)
                    s_load.bump(sy.dma_start(Ypin[:], scr_Ypart[c * 128:(c + 1) * 128, :]), 16)
                    sy.wait_ge(s_ve.h, b_ve + PV * c + 1)
                    for r, t in [(0, hi_bf), (1, lo_bf), (2, nhi_bf), (3, nlo_bf)]:
                        s_spill.bump(sy.dma_start(
                            scr_flat[r:r + 1, :].rearrange("a (h k) -> (a h) k", h=128),
                            t[:]), 16)
                    sy.wait_ge(s_spill.h, b_sp + PS * c + 64)
                    for p in range(4):
                        if p >= 1:
                            sy.wait_ge(s_tp.h, b_tp + PT * c + 12 * p + 7)
                        for li, r, half in [(0, 2, 0), (1, 3, 0), (2, 0, 1), (3, 1, 1)]:
                            s_load.bump(sy.dma_start(
                                Lwin[li:li + 1, half * 4096:half * 4096 + 4096],
                                scr_flat[r:r + 1, 4096 * p:4096 * (p + 1)]), 16)
                    sy.wait_ge(s_ve.h, b_ve + PV * (c + 1))
                    s_spill.bump(sy.dma_start(scr_Ypart[c * 128:(c + 1) * 128, :], Ypin[:]), 16)
                    if debug and c == NCH - 1:
                        s_spill.bump(sy.dma_start(dbg_L, ebuf[0][:]), 16)
                        s_spill.bump(sy.dma_start(dbg_Lw, Lwin[:]), 16)
                        s_spill.bump(sy.dma_start(dbg_Gm, GmT_sb[:]), 16)

            @blk.tensor
            def _(te):
                for c in range(NCH):
                    base_e = b_ev + PE_ * c
                    te.wait_ge(s_load.h, b_load + PL * c + 48)
                    for gi2 in range(8):
                        if gi2 >= 1:
                            te.wait_ge(s_ev.h, base_e + gi2)
                        elif c >= 1:
                            te.wait_ge(s_ev.h, b_ev + PE_ * (c - 1) + 8)
                        s_tp.bump(te.matmul(psG[:], Bbuf2[:, gi2 * 128:(gi2 + 1) * 128],
                                            cT_all[:, (gi2 * NCH + c) * 128:(gi2 * NCH + c + 1) * 128],
                                            start=True, stop=True), 1)
                    dup_maskr = maskr[:].unsqueeze(1).broadcast_to([128, 4, 128])
                    for yb in range(16):
                        for sub in range(2):
                            hb = 2 * yb + sub
                            p = hb // 8
                            te.wait_ge(s_load.h, b_load + PL * c + 64 + 64 * (p + 1))
                            if hb >= 2:
                                te.wait_ge(s_ev.h, base_e + 8 + hb - 1)
                            elif c >= 1:
                                te.wait_ge(s_ev.h, b_ev + PE_ * (c - 1) + 40)
                            mm = None
                            for hh in range(4):
                                h = 4 * hb + hh
                                off = (h - 32 * p) * 128
                                mm = te.matmul(psL[hb % 2][:, hh * 128:(hh + 1) * 128],
                                               Lwin[0:4, off:off + 128],
                                               Lwin[0:4, 4096 + off:4096 + off + 128],
                                               start=(hh == 0), stop=False)
                            mm = te.matmul(psL[hb % 2][:], maskl[:], dup_maskr,
                                           start=False, stop=True)
                            s_tp.bump(mm, 1)
                        te.wait_ge(s_ve.h, b_ve + PV * c + 3 * yb + 3)
                        te.wait_ge(s_load.h, b_load + PL * c + 64)
                        if yb >= 2:
                            te.wait_ge(s_ve.h, b_ve + PV * c + 3 * yb - 2)
                        elif c >= 1:
                            te.wait_ge(s_ve.h, b_ve + PV * c)
                        mm = None
                        for hh in range(8):
                            h = 8 * yb + hh
                            zb = (2 * yb + hh // 4) % 2
                            mm = te.matmul(psY[yb % 2][:, hh * 64:(hh + 1) * 64],
                                           Zbuf[zb][:, (h % 4) * 128:(h % 4) * 128 + 128],
                                           xsb[:, h * 64:(h + 1) * 64],
                                           start=(hh == 0), stop=(hh == 7))
                        s_tp.bump(mm, 1)

            @blk.scalar
            def _(s):
                for c in range(NCH):
                    base_t = b_tp + PT * c
                    for gi2 in range(8):
                        if c >= 1 and gi2 == 0:
                            s.wait_ge(s_ve.h, b_ve + PV * c)
                        s.wait_ge(s_tp.h, base_t + gi2 + 1)
                        s_ev.bump(s.copy(GmT_sb[:, gi2 * 128:(gi2 + 1) * 128], psG[:]), 1)
                    for hb in range(32):
                        s.wait_ge(s_tp.h, base_t + 8 + (hb // 2) * 3 + (hb % 2) + 1)
                        if hb >= 2:
                            s.wait_ge(s_ve.h, b_ve + PV * c + 2 + 3 * ((hb - 2) // 2) + ((hb - 2) % 2))
                        s_ev.bump(s.activation(ebuf[hb % 2][:], psL[hb % 2][:], AF.Exp), 1)

            @blk.vector
            def _(v):
                for c in range(NCH):
                    base_e = b_ev + PE_ * c
                    base_t = b_tp + PT * c
                    if c >= 1:
                        v.wait_ge(s_spill.h, b_sp + PS * (c - 1) + 64)
                    basecol = zcol[:] if c == 0 else cumAH[:, c * 128 - 1:c * 128]
                    v.tensor_scalar(crebuf[:], cumAH[:, c * 128:(c + 1) * 128], basecol,
                                    None, ALU.subtract)
                    v.drain()
                    v.tensor_copy(hi_bf[:], crebuf[:])
                    v.drain()
                    v.tensor_tensor(lo_bf[:], crebuf[:], hi_bf[:], ALU.subtract)
                    v.tensor_scalar(nhi_bf[:], hi_bf[:], -1.0, None, ALU.mult)
                    v.drain()
                    v.tensor_scalar(nlo_bf[:], lo_bf[:], -1.0, None, ALU.mult)
                    v.drain()
                    s_ve.bump(v.nop(), 1)
                    for yb in range(16):
                        for sub in range(2):
                            hb = 2 * yb + sub
                            gi2 = hb // 4
                            v.wait_ge(s_ev.h, base_e + 8 + hb + 1)
                            if hb >= 2:
                                v.wait_ge(s_tp.h, base_t + 8 + 3 * (hb // 2 - 1) + 3)
                            dupg = GmT_sb[:, gi2 * 128:(gi2 + 1) * 128].unsqueeze(1) \
                                .broadcast_to([128, 4, 128])
                            s_ve.bump(v.tensor_tensor(
                                Zbuf[hb % 2][:].rearrange("p (a t) -> p a t", a=4),
                                ebuf[hb % 2][:].rearrange("p (a t) -> p a t", a=4),
                                dupg, ALU.mult), 1)
                        v.wait_ge(s_tp.h, base_t + 8 + 3 * (yb + 1))
                        s_ve.bump(v.tensor_tensor(Ypin[:, yb * 512:(yb + 1) * 512],
                                                  psY[yb % 2][:],
                                                  Ypin[:, yb * 512:(yb + 1) * 512],
                                                  ALU.add), 1)

    with nc.Block("drain") as blk:
        @blk.sync
        def _(sy):
            sy.wait_ge(s_spill.h, s_spill.n)

        @blk.gpsimd
        def _(g):
            g.wait_ge(s_cast.h, s_cast.n)
            g.wait_ge(s_out.h, s_out.n)

    es.close()
    return nc


def _host_prepare(inputs):
    hs = np.ascontiguousarray(np.asarray(inputs["L0_attn_hidden_states"])[0])
    Win = np.asarray(inputs["L0_attn_in_proj_w"])
    WinT = np.ascontiguousarray(Win.T)
    conv_w = np.asarray(inputs["L0_attn_conv_w"])
    conv_b = np.asarray(inputs["L0_attn_conv_b"])
    D = np.asarray(inputs["L0_attn_D"])

    shared = {
        "wt_xbcdt": np.ascontiguousarray(WinT[:, DIN:]),
        "wt_gate": np.ascontiguousarray(WinT[:, :DIN]),
        "wt_out": np.ascontiguousarray(np.asarray(inputs["L0_attn_out_proj_w"]).T),
        "wt_base": np.ascontiguousarray(np.asarray(inputs["L0_ffn_base_w"]).T),
        "wt_gatew": np.ascontiguousarray(np.asarray(inputs["L0_ffn_gate_w"]).T),
        "wt_lA": np.ascontiguousarray(
            np.asarray(inputs["L0_ffn_lora_A"]).transpose(2, 0, 1).reshape(DM, 128)),
        "wt_lB": np.ascontiguousarray(
            np.asarray(inputs["L0_ffn_lora_B"]).transpose(0, 2, 1).reshape(128, DM)),
        "cw_in": np.ascontiguousarray(
            conv_w.reshape(80, 128, 4).transpose(1, 0, 2).reshape(128, 320)),
        "cb_in": np.ascontiguousarray(conv_b.reshape(80, 128).T),
        "dtb_in": np.ascontiguousarray(np.asarray(inputs["L0_attn_dt_bias"]).reshape(128, 1)),
        "alog_in": np.ascontiguousarray(np.asarray(inputs["L0_attn_A_log"]).reshape(128, 1)),
        "nw_in": np.ascontiguousarray(np.asarray(inputs["L0_attn_norm_w"]).reshape(1, DIN)),
        "bb_in": np.ascontiguousarray(np.asarray(inputs["L0_ffn_base_b"]).reshape(1, DM)),
        "ident_in": np.eye(128, dtype=bfdt),
        "identf_in": np.eye(128, dtype=np.float32),
        "emat_in": np.ascontiguousarray(
            (np.arange(128)[None, :] // 16 == np.arange(8)[:, None]).astype(bfdt)),
    }
    t = np.arange(128)
    shared["maskl_in"] = np.ascontiguousarray(
        np.where(t[:, None] <= t[None, :], np.float32(MASKBIG), np.float32(0)).astype(bfdt))
    shared["maskr_in"] = np.ascontiguousarray((t[:, None] > t[None, :]).astype(bfdt))
    shared["drow_in"] = np.ascontiguousarray(D.reshape(1, 128).astype(np.float32))
    dd = np.zeros((DIN, 128), np.float32)
    ch = np.arange(DIN)
    dd[ch, ch % 128] = D[ch // 64]
    shared["ddiag_in"] = dd

    per_core = []
    for i in range(NCORE):
        halo = np.zeros((3, DM), np.float32) if i == 0 else hs[1024 * i - 3:1024 * i]
        m = dict(shared)
        m["x_in"] = np.ascontiguousarray(
            np.concatenate([halo, hs[1024 * i:1024 * (i + 1)]], 0))
        m["m8_in"] = np.ascontiguousarray((np.arange(8)[None, :] < i).astype(np.float32))
        m["m8c_in"] = np.ascontiguousarray(1.0 - m["m8_in"])
        per_core.append(m)
    return per_core


_P2_KEYS = ["x_in", "wt_out", "wt_base", "wt_gatew", "wt_lA", "wt_lB",
            "nw_in", "bb_in", "ident_in", "emat_in", "m8_in", "m8c_in",
            "wt_xbcdt", "wt_gate", "cw_in", "cb_in", "dtb_in", "alog_in",
            "ddiag_in", "identf_in", "drow_in", "maskl_in", "maskr_in"]


def kernel(**inputs):
    per_core = _host_prepare(inputs)
    nc1 = build_program(part=1, debug=bool(_DEBUG))
    res1 = run_bass_kernel_spmd(nc1, per_core, core_ids=list(range(NCORE)))
    if _DEBUG:
        kernel.debug_results = res1.results
    r_all = np.concatenate([np.asarray(res1.results[j]["cc_in"]) for j in range(NCORE)],
                           axis=0)
    per_core2 = []
    for i in range(NCORE):
        r = res1.results[i]
        m = {k: per_core[i][k] for k in _P2_KEYS}
        m["cc_out"] = r_all
        for name in ["scr_Ypart", "scr_S", "scr_gate", "xp_cT", "xp_sdecT", "xp_cdrow"]:
            m[name] = np.asarray(r[name])
        per_core2.append(m)
    nc2 = build_program(part=2, debug=bool(_DEBUG))
    res2 = run_bass_kernel_spmd(nc2, per_core2, core_ids=list(range(NCORE)))
    if _DEBUG:
        kernel.debug_results2 = res2.results
    out = np.concatenate([np.asarray(r["out"]) for r in res2.results], axis=0)
    return out.reshape(1, 8192, DM)


